# version 02
# baseline (speedup 1.0000x reference)
class Node:
    __slots__ = ("eng", "fn", "deps", "is_dma", "chain", "inc", "sem", "val",
                 "tag", "idx", "amt", "is_cc")

    def __init__(self, eng, fn, deps, is_dma, chain, tag, amt=None):
        self.eng = eng
        self.fn = fn
        self.deps = [d for d in deps if d is not None]
        self.is_dma = is_dma
        self.chain = chain
        self.inc = False
        self.sem = None
        self.val = None
        self.tag = tag
        self.idx = None
        self.amt = amt if amt is not None else (16 if is_dma else 1)
        self.is_cc = False


class Flow:
    def __init__(self, nc):
        self.nc = nc
        self.nodes = {e: [] for e in ("pe", "dve", "act", "pool", "sync")}
        self.sems = {}
        for e in ("pe", "dve", "act", "pool"):
            self.sems[("c", e)] = nc.alloc_semaphore(f"fc_{e}")
        self.chain_sems = {}
        self.chain_last = {}
        self.frozen = False

    def op(self, eng, fn, deps=(), tag=""):
        n = Node(eng, fn, deps, False, None, tag)
        self.nodes[eng].append(n)
        return n

    def dma(self, eng, fn, deps=(), tag="", chain=None):
        if chain is None:
            chain = tag
        chain = (eng, chain)
        n = Node(eng, fn, deps, True, chain, tag)
        self.nodes[eng].append(n)
        self.chain_last[chain] = n
        return n

    def cc(self, fn, deps=(), tag=""):
        chain = ("pool", "cc")
        n = Node("pool", fn, deps, True, chain, tag, amt=1)
        n.is_cc = True
        self.nodes["pool"].append(n)
        self.chain_last[chain] = n
        return n

    def _chain_sem(self, chain):
        if chain not in self.chain_sems:
            self.chain_sems[chain] = self.nc.alloc_semaphore(
                f"fch_{len(self.chain_sems)}")
        return self.chain_sems[chain]

    def finalize(self):
        assert not self.frozen
        self.frozen = True
        for lst in self.nodes.values():
            for n in lst:
                if n.is_dma:
                    n.inc = True
                for d in n.deps:
                    d.inc = True
        for lst in self.nodes.values():
            if lst:
                lst[-1].inc = True
        ccnt = {}
        dcnt = {}
        for eng, lst in self.nodes.items():
            for i, n in enumerate(lst):
                n.idx = i
                if n.is_dma:
                    n.sem = self._chain_sem(n.chain)
                    dcnt[n.chain] = dcnt.get(n.chain, 0) + n.amt
                    n.val = dcnt[n.chain]
                else:
                    key = ("c", eng)
                    n.sem = self.sems[key]
                    if n.inc:
                        ccnt[key] = ccnt.get(key, 0) + 1
                        n.val = ccnt[key]
        self.totals = {}
        for eng, lst in self.nodes.items():
            for n in lst:
                if n.inc:
                    self.totals[n.sem] = max(self.totals.get(n.sem, 0), n.val)

    def emit(self, eng, handle):
        assert self.frozen
        hw = {}
        for n in self.nodes[eng]:
            want = {}
            wtag = {}
            for d in n.deps:
                assert d.inc and d.val is not None, (d.tag, n.tag)
                if d.val > want.get(d.sem, -1):
                    want[d.sem] = d.val
                    wtag[d.sem] = d.tag
            for sem, val in want.items():
                if hw.get(sem, -1) >= val:
                    continue
                handle.wait_ge(sem, val).annotate(f"w[{n.tag}<-{wtag[sem]}]")
                hw[sem] = val
            instr = n.fn(handle)
            instr is not None and instr.annotate(n.tag)
            if n.inc:
                assert instr is not None, f"node {n.tag} must return its instruction"
                if n.is_cc:
                    instr.then_inc(n.sem)
                else:
                    instr.then_inc(n.sem, n.amt)

    def checkpoint(self):
        return [lst[-1] for lst in self.nodes.values() if lst] + list(
            self.chain_last.values())



import numpy as np
import ml_dtypes

import os
os.environ.setdefault("TRN_ALLOW_ICI_ALL_TO_ALL", "1")

import concourse.bass as bass
import concourse.mybir as mybir
from concourse.aot_env import allow_ici_all_to_all

F32 = mybir.dt.float32
BF16 = mybir.dt.bfloat16
ALU = mybir.AluOpType
AF = mybir.ActivationFunctionType
BF = ml_dtypes.bfloat16


class CFG:
    def __init__(self, T=8192, D=1024, NEXP=64):
        self.T, self.D, self.NB = T, D, 4
        self.H, self.DM, self.NPAIR = 16, 64, 4
        self.NEXP, self.P = NEXP, 32
        self.NP = NEXP * 32
        self.Town = T // 8
        self.DCH = D // 128
        self.NT = T // 128
        self.NMT = self.Town // 128
        self.NBLK = T // 512
        self.NPT = self.NP // 128
        self.NPC = self.NP // 512
        self.NE = NEXP // 8
        self.HCH = 8
        self.EH = max(1, D // 512)
        self.EW = min(512, D)
        assert self.Town % 128 == 0 and self.NP % 512 == 0


def build_consts(cfg):
    c = np.zeros((128, 128 + 4 * 512 + 2), np.float32)
    c[:, :128] = np.eye(128, dtype=np.float32)
    i = np.arange(128)[:, None]
    j = np.arange(512)[None, :]
    for k in range(4):
        c[:, 128 + k * 512: 128 + (k + 1) * 512] = (i - j + 128 * k >= 0)
    c[0:64, 128 + 2048] = 1.0
    c[64:128, 128 + 2048 + 1] = 1.0
    return c


def build_sel(cfg):
    s = np.zeros((6, 128), np.float32)
    s[0, 0:64] = 1.0
    s[1, 64:128] = 1.0
    s[2, 0:64] = 1.0
    s[3, 64:128] = 1.0
    s[4, :] = 1.0
    s[5, :] = 1.0
    return s


class Builder:
    def __init__(self, cfg: CFG):
        self.cfg = cfg
        c = cfg
        self.TOK = c.NB * c.Town
        self.NCH = c.T // 512
        self.nc = bass.Bass("TRN2", target_bir_lowering=False, debug=False)
        self.fl = Flow(self.nc)
        self._dram()
        self._sbuf()
        self._psum()

    def _dram(self):
        nc, c = self.nc, self.cfg
        T, D, NP, NE, Town = c.T, c.D, c.NP, c.NE, c.Town

        def dr(name, shape, dt, kind="Internal"):
            setattr(self, name + "_d", nc.dram_tensor(name, shape, dt, kind=kind).ap())

        dr("xb", [T, D], F32, "ExternalInput")
        dr("xo", [self.TOK, D], F32, "ExternalInput")
        dr("wk", [D, 512], F32, "ExternalInput")
        dr("wq", [D, 512], F32, "ExternalInput")
        dr("wv", [D, 512], F32, "ExternalInput")
        dr("wb", [D, 8], F32, "ExternalInput")
        dr("wo", [1024, D], F32, "ExternalInput")
        dr("phi", [D, NP], F32, "ExternalInput")
        dr("wexp", [NE * D, D], F32, "ExternalInput")
        dr("bexp", [NE * 32, D], F32, "ExternalInput")
        dr("cst", [128, 2178], F32, "ExternalInput")
        dr("sel", [6, 128], F32, "ExternalInput")
        dr("out", [self.TOK, D], F32, "ExternalOutput")

        dr("qhat", [c.NPAIR * 128, T], BF16)
        dr("xfs", [128, self.NCH * c.DCH * 512], BF16)
        dr("e_sp", [self.TOK, NP], BF16)
        dr("disp", [self.TOK, NP], BF16)
        dr("sl_in", [NP, c.NB * D], F32)
        dr("sl_own", [NP // 8, c.NB * D], F32)
        dr("y_in", [NP // 8, c.NB * D], BF16)
        dr("y_full", [NP, c.NB * D], BF16)
        dr("oh_send", [8 * c.NPAIR * 128, Town], BF16)
        dr("oh_recv", [8 * c.NPAIR * 128, Town], BF16)

    def _sbuf(self):
        nc, c = self.nc, self.cfg
        D, NP, T, Town = c.D, c.NP, c.T, c.Town
        PB = 128
        self.P1_XNT = PB
        self.P1_PHI = self.P1_XNT + c.NB * c.NMT * D
        self.P1_DB = self.P1_PHI + c.DCH * NP
        self.P1_XF = self.P1_DB + c.NB * 512
        self.P1_E = self.P1_XF + c.NB * D
        P1_END = self.P1_E + c.NB * 512

        self.P2_KH = PB
        self.P2_VH = self.P2_KH + c.NPAIR * T
        REG_A = self.P2_VH + c.NPAIR * T
        self.REG_A = REG_A
        self.P2_W = REG_A
        self.P2_XT = self.P2_W + 2 * c.DCH * 512
        self.P2_XF = self.P2_XT + 2 * D
        self.P2_SQ = self.P2_XF + c.DCH * 512
        self.P2_QE = self.P2_SQ + 512
        P2_END = self.P2_QE + 512

        self.Y_W = REG_A
        self.Y_SLT = self.Y_W + c.DCH * D
        self.Y_SL = self.Y_SLT + c.DCH * 128
        self.Y_EV = self.Y_SL + D
        Y_END = self.Y_EV + D

        self.P3_SSB = REG_A
        self.P3_QT = self.P3_SSB + 8 * 1024
        self.P3_OE = self.P3_QT + 1024
        P3_END = self.P3_OE + 1024

        self.P4_Y = PB
        self.P4_OH = self.P4_Y + c.NPT * D
        self.P4_WO = self.P4_OH + c.HCH * Town
        self.P4_CT = self.P4_WO + c.HCH * D
        self.P4_CB = self.P4_CT + c.NPT * 128
        self.P4_ER = self.P4_CB + NP
        P4_END = self.P4_ER + 2 * NP

        ABF = max(P1_END, P2_END, Y_END, P3_END, P4_END)
        o = 0
        def f(n):
            nonlocal o
            r = o
            o += n
            return r
        self.F_CST = f(2178)
        self.F_DC = f(c.NB * c.NMT)
        self.F_DCR = f(c.NB * c.NMT)
        self.F_LN = f(16)
        self.F_XLD = f(2 * D)
        self.F_WST = f(2048)
        self.F_DD = f(512)
        self.F_DR = f(512)
        self.F_SLE = f(max(1024, D))
        self.F_NB = f(512)
        self.F_BET = f(c.NT * 8)
        self.F_G = f(D)
        self.F_YB = self.F_G
        self.F_FO = f(D)
        AF_COLS = o
        assert ABF * 2 + AF_COLS * 4 <= 206 * 1024, (ABF, AF_COLS)

        self.abf = nc.alloc_sbuf_tensor("abf", [128, ABF], BF16)
        self.af = nc.alloc_sbuf_tensor("af", [128, AF_COLS], F32)
        self.selb = nc.alloc_sbuf_tensor("selb", [6, 128], BF16)
        self.self_f = nc.alloc_sbuf_tensor("self_f", [6, 128], F32)
        self.hi2 = nc.alloc_sbuf_tensor("hi2", [2, 512], BF16)
        self.lo2 = nc.alloc_sbuf_tensor("lo2", [2, 512], BF16)
        self.i128b = self.abf[:, 0:128]
        self.i128f = self.af[:, self.F_CST:self.F_CST + 128]
        self.ivf = self.af[0:2, self.F_NB:self.F_NB + 512]

    def bf(self, off, w):
        return self.abf[:, off:off + w]

    def ff(self, off, w):
        return self.af[:, off:off + w]

    def mask(self, k):
        return self.af[:, self.F_CST + 128 + k * 512: self.F_CST + 128 + (k + 1) * 512]

    def ones2f(self):
        return self.af[:, self.F_CST + 2176: self.F_CST + 2178]

    def _psum(self):
        self.psA = self.nc.alloc_psum_tensor("psA", [128, 6 * 512], F32)
        self.psB = self.nc.alloc_psum_tensor("psB", [128, 2048], BF16)

    def bank(self, i, w=512, o=0):
        return self.psA[:, i * 512 + o: i * 512 + o + w]

    def p0(self):
        import os
        lvl = int(os.environ.get("P0LVL", "9"))
        fl, c = self.fl, self.cfg
        OP, DMA = fl.op, fl.dma
        n_cst = DMA("sync", lambda e: e.dma_start(
            out=self.ff(self.F_CST, 2178), in_=self.cst_d), tag="cst", chain="cst")
        n_sel = DMA("sync", lambda e: e.dma_start(
            out=self.self_f[:], in_=self.sel_d), tag="sel", chain="sel")
        self.n_cst = n_cst
        if lvl < 2:
            return
        self.n_i128 = OP("pool", lambda e: e.tensor_copy(self.i128b, self.i128f),
                         deps=[n_cst])
        self.n_selb = OP("pool", lambda e: e.tensor_copy(self.selb[:], self.self_f[:]),
                         deps=[n_sel])
        self.ones2b_t = self.nc.alloc_sbuf_tensor("ones2b", [128, 2], BF16)
        self.n_ones2 = OP("pool", lambda e: e.tensor_copy(
            self.ones2b_t[:], self.ones2f()), deps=[n_cst])
        self.eps_ap = self.af[:, self.F_LN + 7:self.F_LN + 8]
        if lvl < 3:
            return
        self.n_eps = OP("pool", lambda e: e.memset(self.eps_ap, 1e-5), tag="eps")
        if lvl < 4:
            return
        self.n_phi = self.cast_weight(self.P1_PHI, self.phi_d, c.NP)

    def cast_weight(self, dst_off, src, ncols, deps0=(), queue="pool", nch=None):
        fl, c = self.fl, self.cfg
        nodes = []
        last_cast_by_buf = [None, None]
        for ch in range(nch if nch is not None else c.DCH):
            st = self.F_WST + (ch % 2) * 1024
            assert ncols <= 1024 or ncols == self.cfg.NP
            sub = [(0, min(1024, ncols))]
            if ncols > 1024:
                sub = [(i, 1024) for i in range(0, ncols, 1024)]
            for (so, sw) in sub:
                dep = [last_cast_by_buf[ch % 2]] if last_cast_by_buf[ch % 2] else list(deps0)
                ld = fl.dma(queue, (lambda e, ch=ch, st=st, so=so, sw=sw: e.dma_start(
                    out=self.ff(st, sw),
                    in_=src[ch * 128:(ch + 1) * 128, so:so + sw])),
                    deps=dep, tag=f"wld", chain=("wst", ch % 2))
                cp = fl.op("pool", (lambda e, ch=ch, st=st, so=so, sw=sw: e.tensor_copy(
                    self.bf(dst_off + ch * ncols + so, sw), self.ff(st, sw))),
                    deps=[ld], tag=f"wcast")
                last_cast_by_buf[ch % 2] = cp
                nodes.append(cp)
        return nodes[-1]

    def ln_tile(self, xoff, xnoff, deps, jnk_off=None, prev=None):
        fl, D = self.fl, self.cfg.D
        OP = fl.op
        x = self.ff(xoff, D)
        jnk = self.ff(self.F_FO, D)
        L = self.F_LN
        s1 = self.af[:, L:L + 1]
        s2 = self.af[:, L + 1:L + 2]
        m = self.af[:, L + 2:L + 3]
        va = self.af[:, L + 3:L + 4]
        rs = self.af[:, L + 4:L + 5]
        t1 = self.af[:, L + 5:L + 6]
        pd = [prev] if prev is not None else []
        a1 = OP("act", lambda e: e.activation(jnk, x, AF.Square, accum_out=s2),
                deps=list(deps) + pd, tag="ln_sq")
        a2 = OP("dve", lambda e: e.tensor_scalar(
            self.bf(xnoff, D), x, 1.0, None, ALU.mult, op1=ALU.add, accum_out=s1),
            deps=list(deps) + pd, tag="ln_s1")
        a3 = OP("dve", lambda e: e.tensor_scalar_mul(m, s1, 1.0 / D), deps=[a2])
        a4 = OP("dve", lambda e: e.tensor_mul(t1, m, m), deps=[a3])
        a5 = OP("dve", lambda e: e.tensor_scalar_mul(va, s2, 1.0 / D), deps=[a1])
        a6 = OP("dve", lambda e: e.tensor_sub(va, va, t1), deps=[a4, a5])
        a7 = OP("act", lambda e: e.activation(rs, va, AF.Sqrt, bias=self.eps_ap),
                deps=[a6, self.n_eps])
        a8 = OP("dve", lambda e: e.reciprocal(rs, rs), deps=[a7])
        a9 = OP("dve", lambda e: e.tensor_scalar(
            self.bf(xnoff, D), x, m, rs, ALU.subtract, ALU.mult),
            deps=[a8, a2], tag="ln_norm")
        return a9

    def p1a(self):
        fl, c = self.fl, self.cfg
        OP, DMA = fl.op, fl.dma
        D, NB, NMT, NPC, NP, DCH = c.D, c.NB, c.NMT, c.NPC, c.NP, c.DCH
        Town = c.Town
        NH = max(1, DCH // 2)
        self.disp_st = {}
        self.e_st = {}
        ln_last = {}
        prev_ln = None
        xnt_nodes = {}
        for mt in range(NMT):
            for b in range(NB):
                j = mt * NB + b
                xoff = self.F_XLD + (j % 2) * D
                ld = DMA("sync", (lambda e, b=b, mt=mt, xoff=xoff: e.dma_start(
                    out=self.ff(xoff, D),
                    in_=self.xo_d[b * Town + mt * 128: b * Town + mt * 128 + 128, :])),
                    deps=[ln_last.get(j - 2)], tag=f"p1x{j}",
                    chain=("xld", j % 2))
                n = self.ln_tile(xoff, self.P1_XNT + (b * NMT + mt) * D,
                                 [ld], prev=prev_ln)
                ln_last[j] = n
                prev_ln = n
                xnt_nodes[(b, mt)] = n
        self.xnt_nodes = xnt_nodes

        tp_bank_last = {0: None, 1: None}
        xf_readers = {b: [] for b in range(NB)}
        e_readers = {b: [] for b in range(NB)}
        mm_bank_last = {0: None, 1: None}
        dc_prev = None
        dd_readers = []
        last_disp = {}
        for mt in range(NMT):
            tp_ev = {}
            for b in range(NB):
                xn_off = self.P1_XNT + (b * NMT + mt) * D
                halves = {0: [], 1: []}
                for cc in range(DCH):
                    h = 0 if cc < NH else 1
                    n = OP("pe", (lambda e, cc=cc, h=h, xn_off=xn_off: e.transpose(
                        self.psB[:, h * 1024 + (cc % NH) * 128:
                                 h * 1024 + (cc % NH) * 128 + 128],
                        self.bf(xn_off + cc * 128, 128), self.i128b)),
                        deps=[xnt_nodes[(b, mt)], tp_bank_last[h], self.n_i128],
                        tag=f"tp{mt}_{b}_{cc}")
                    halves[h].append(n)
                for h in (0, 1):
                    if not halves[h]:
                        continue
                    w = len(halves[h]) * 128
                    n = OP("dve", (lambda e, b=b, h=h, w=w: e.tensor_copy(
                        self.bf(self.P1_XF + b * D + h * NH * 128, w),
                        self.psB[:, h * 1024: h * 1024 + w])),
                        deps=halves[h] + xf_readers[b], tag=f"tpe{mt}_{b}_{h}")
                    tp_bank_last[h] = n
                    tp_ev[(b, h)] = n
                xf_readers[b] = []
            for npc in range(NPC):
                exps = {}
                for b in range(NB):
                    bk = (npc * NB + b) % 2
                    mms = []
                    for cc in range(DCH):
                        dep = [tp_ev[(b, 0 if cc < NH else 1)], self.n_phi,
                               mm_bank_last[bk]] if cc == 0 else [
                               tp_ev[(b, 0 if cc < NH else 1)]]
                        mm = OP("pe", (lambda e, b=b, cc=cc, npc=npc, bk=bk: e.matmul(
                            self.bank(bk),
                            lhsT=self.bf(self.P1_XF + b * D + cc * 128, 128),
                            rhs=self.bf(self.P1_PHI + cc * NP + npc * 512, 512),
                            start=(cc == 0), stop=(cc == DCH - 1),
                            skip_group_check=True)),
                            deps=dep, tag=f"lg{mt}_{b}_{npc}_{cc}")
                        mms.append(mm)
                    xf_readers[b].extend(mms)
                    tmp = self.af[:, self.F_LN + 6:self.F_LN + 7]
                    ex = OP("act", (lambda e, b=b, bk=bk: e.activation(
                        self.bf(self.P1_E + b * 512, 512), self.bank(bk),
                        AF.Exp, accum_out=tmp)),
                        deps=[mms[-1]] + e_readers[b] + [dc_prev],
                        tag=f"ex{mt}_{b}_{npc}")
                    e_readers[b] = []
                    exps[b] = ex
                    mm_bank_last[bk] = ex
                    dcc = self.af[:, self.F_DC + b * NMT + mt:
                                  self.F_DC + b * NMT + mt + 1]
                    if npc == 0:
                        dc = OP("dve", (lambda e, dcc=dcc: e.tensor_copy(dcc, tmp)),
                                deps=[ex, dc_prev], tag="dc")
                    else:
                        dc = OP("dve", (lambda e, dcc=dcc: e.tensor_add(dcc, dcc, tmp)),
                                deps=[ex, dc_prev], tag="dc")
                    dc_prev = dc
                dd = self.ff(self.F_DD, 512)
                dr = self.ff(self.F_DR, 512)
                d1 = OP("dve", lambda e: e.tensor_add(
                    dd, self.bf(self.P1_E + 0, 512), self.bf(self.P1_E + 512, 512)),
                    deps=[exps[0], exps[1]] + dd_readers, tag="dd1")
                d2 = OP("dve", lambda e: e.tensor_add(
                    dd, dd, self.bf(self.P1_E + 1024, 512)), deps=[d1, exps[2]])
                d3 = OP("dve", lambda e: e.tensor_add(
                    dd, dd, self.bf(self.P1_E + 1536, 512)), deps=[d2, exps[3]])
                d4 = OP("dve", lambda e: e.reciprocal(dr, dd),
                        deps=[d3] + getattr(self, "_dr_readers", []))
                dd_readers = [d4]
                self._dr_readers = []
                dd_r = {0: d1, 1: d1, 2: d2, 3: d3}
                for b in range(NB):
                    dsp = OP("dve", (lambda e, b=b: e.tensor_tensor(
                        self.bf(self.P1_DB + b * 512, 512),
                        self.bf(self.P1_E + b * 512, 512), dr, ALU.mult)),
                        deps=[d4, last_disp.get(b)], tag=f"dsp{mt}_{b}_{npc}")
                    st1 = DMA("pool", (lambda e, b=b, mt=mt, npc=npc: e.dma_start(
                        out=self.disp_d[b * Town + mt * 128: b * Town + mt * 128 + 128,
                                        npc * 512:(npc + 1) * 512],
                        in_=self.bf(self.P1_DB + b * 512, 512))),
                        deps=[dsp], tag=f"dspst{mt}_{b}_{npc}",
                        chain=("dst", b))
                    st2 = DMA("pool", (lambda e, b=b, mt=mt, npc=npc: e.dma_start(
                        out=self.e_sp_d[b * Town + mt * 128: b * Town + mt * 128 + 128,
                                        npc * 512:(npc + 1) * 512],
                        in_=self.bf(self.P1_E + b * 512, 512))),
                        deps=[exps[b]], tag=f"est{mt}_{b}_{npc}",
                        chain=("est", b))
                    last_disp[b] = st1
                    self._dr_readers.append(dsp)
                    e_readers[b] = [dd_r[b], dsp, st2]
                    self.disp_st[(b, mt, npc)] = st1
                    self.e_st.setdefault((b, mt), []).append(st2)
        self.p1a_done = fl.checkpoint()

    def p1b(self):
        fl, c = self.fl, self.cfg
        OP, DMA = fl.op, fl.dma
        D, NB, NMT, NPC, NP = c.D, c.NB, c.NMT, c.NPC, c.NP
        Town = c.Town
        store_nodes = []
        ld_last = {0: None, 1: None}
        bank_free = {i: None for i in range(6)}
        ev_buf_free = {0: None, 1: None}
        j = 0
        sweep = 0
        bar = self.p1a_done
        EH, EW = c.EH, c.EW
        for b in range(NB):
            for npq in range(NPC):
                for dh in range(EH):
                    banks = [(4 * sweep + t) % 6 for t in range(4)]
                    sweep += 1
                    mm_last = {}
                    for mt in range(NMT):
                        sl = j % 2
                        ld = DMA("sync",
                                 (lambda e, b=b, npq=npq, mt=mt, sl=sl: e.dma_start(
                            out=self.bf(self.P1_DB + sl * 512, 512),
                            in_=self.disp_d[
                                b * Town + mt * 128: b * Town + mt * 128 + 128,
                                npq * 512:(npq + 1) * 512])),
                            deps=[ld_last[sl], self.disp_st[(b, mt, npq)]] + bar,
                            tag=f"dld{j}", chain=("dld", sl))
                        bar = []
                        j += 1
                        for nt in range(4):
                            bk = banks[nt]
                            dep = [ld]
                            if mt == 0:
                                dep += [bank_free[bk], self.xnt_nodes[(b, 0)]]
                            mm = OP("pe", (lambda e, b=b, mt=mt, nt=nt, dh=dh, sl=sl,
                                           bk=bk: e.matmul(
                                self.bank(bk, w=EW),
                                lhsT=self.bf(self.P1_DB + sl * 512 + nt * 128, 128),
                                rhs=self.bf(self.P1_XNT + (b * NMT + mt) * D + dh * EW,
                                            EW),
                                start=(mt == 0), stop=(mt == NMT - 1),
                                skip_group_check=True)),
                                deps=dep, tag=f"sl{b}_{npq}_{mt}_{nt}_{dh}")
                            mm_last[nt] = mm
                        ld_last[sl] = mm_last[3]
                    for nt in range(4):
                        bk = banks[nt]
                        eb = (sweep * 4 + nt) % 2
                        ev = OP("dve", (lambda e, bk=bk, eb=eb: e.tensor_copy(
                            self.ff(self.F_SLE + eb * 512, EW),
                            self.bank(bk, w=EW))),
                            deps=[mm_last[nt], ev_buf_free[eb]],
                            tag=f"slev{b}_{npq}_{dh}_{nt}")
                        st = DMA("pool", (lambda e, b=b, npq=npq, nt=nt, dh=dh, eb=eb:
                                          e.dma_start(
                            out=self.sl_in_d[
                                npq * 512 + nt * 128: npq * 512 + nt * 128 + 128,
                                b * D + dh * EW: b * D + dh * EW + EW],
                            in_=self.ff(self.F_SLE + eb * 512, EW))),
                            deps=[ev], tag=f"slst{b}_{npq}_{dh}_{nt}",
                            chain=("slst", eb))
                        bank_free[bk] = ev
                        ev_buf_free[eb] = st
                        store_nodes.append(st)
        import os
        if os.environ.get("RSNODEP"):
            self.n_rs = fl.cc(lambda e: e.collective_compute(
                "ReduceScatter", ALU.add, replica_groups=[list(range(8))],
                ins=[self.sl_in_d], outs=[self.sl_own_d]),
                deps=[], tag="RS")
        elif os.environ.get("NORS"):
            self.n_rs = fl.dma("pool", lambda e: e.dma_start(
                out=self.sl_own_d, in_=self.sl_in_d[0:self.cfg.NP // 8, :]),
                deps=store_nodes, tag="RS", chain="ccx")
        else:
            self.n_rs = fl.cc(lambda e: e.collective_compute(
                "ReduceScatter", ALU.add, replica_groups=[list(range(8))],
                ins=[self.sl_in_d], outs=[self.sl_own_d]),
                deps=store_nodes, tag="RS")
        self.p1b_done = fl.checkpoint()

    def p2(self):
        fl, c = self.fl, self.cfg
        OP, DMA = fl.op, fl.dma
        D, DCH, T, NPAIR = c.D, c.DCH, c.T, c.NPAIR
        NCH = self.NCH
        NH = max(1, DCH // 2)
        bar = self.p1b_done
        nq = self.cast_weight(self.P2_W, self.wq_d, 512, deps0=bar)
        nk = self.cast_weight(self.P2_W + DCH * 512, self.wk_d, 512, deps0=bar)
        prev_ln = None
        xld_last = {0: None, 1: None}
        tp_bank_last = {0: None, 1: None}
        xf_readers = []
        psfree = {i: None for i in range(6)}
        kh_nodes = []
        qh_store = {}
        nprev = {"ss": None, "h2": None, "bc": None, "fn": None}
        self.qh_st = {}
        self.xf_ready = {}
        for ch in range(NCH):
            tpe = {}
            for tt in range(4):
                j = ch * 4 + tt
                sl = j % 2
                ld = DMA("sync", (lambda e, j=j, sl=sl: e.dma_start(
                    out=self.ff(self.F_XLD + sl * D, D),
                    in_=self.xb_d[j * 128:(j + 1) * 128, :])),
                    deps=[xld_last[sl]] + bar, tag=f"p2x{j}",
                    chain=("xld", sl))
                bar = []
                n = self.ln_tile(self.F_XLD + sl * D, self.P2_XT + (tt % 2) * D,
                                 [ld], prev=prev_ln)
                prev_ln = n
                xld_last[sl] = n
                halves = {0: [], 1: []}
                for cc in range(DCH):
                    h = 0 if cc < NH else 1
                    tn = OP("pe", (lambda e, cc=cc, h=h, tt=tt: e.transpose(
                        self.psB[:, h * 1024 + (cc % NH) * 128:
                                 h * 1024 + (cc % NH) * 128 + 128],
                        self.bf(self.P2_XT + (tt % 2) * D + cc * 128, 128),
                        self.i128b)),
                        deps=[n, tp_bank_last[h], self.n_i128], tag=f"p2tp{j}_{cc}")
                    halves[h].append(tn)
                for h in (0, 1):
                    if not halves[h]:
                        continue
                    xfv = self.abf[:, self.P2_XF:self.P2_XF + DCH * 512].rearrange(
                        "p (c m) -> p c m", m=512)[:, h * NH:(h + 1) * NH,
                                                   tt * 128:tt * 128 + 128]
                    en = OP("dve", (lambda e, h=h, xfv=xfv: e.tensor_copy(
                        xfv, self.psB[:, h * 1024: h * 1024 + NH * 128].rearrange(
                            "p (c m) -> p c m", m=128))),
                        deps=halves[h] + xf_readers, tag=f"p2te{j}_{h}")
                    tp_bank_last[h] = en
                    tpe[(tt, h)] = en
            xf_readers = []
            tp_all = list(tpe.values())
            sp = DMA("pool", (lambda e, ch=ch: e.dma_start(
                out=self.xfs_d[:, ch * DCH * 512:(ch + 1) * DCH * 512],
                in_=self.bf(self.P2_XF, DCH * 512))),
                deps=tp_all, tag=f"xfsp{ch}", chain="xfs")
            self.xf_ready[ch] = sp
            xf_readers.append(sp)
            for pair in range(NPAIR):
                for (woff, pbank, sbank, bbank, is_q) in (
                        (self.P2_W, 0, 2, 3, True),
                        (self.P2_W + DCH * 512, 1, 4, 5, False)):
                    mms = []
                    for cc in range(DCH):
                        dep = [tp_all[0] if cc else None, nq, nk, psfree[pbank]]
                        dep = (tp_all + [nq, nk, psfree[pbank]]) if cc == 0 else []
                        mm = OP("pe", (lambda e, cc=cc, pair=pair, woff=woff,
                                       pbank=pbank: e.matmul(
                            self.bank(pbank),
                            lhsT=self.bf(woff + cc * 512 + pair * 128, 128),
                            rhs=self.bf(self.P2_XF + cc * 512, 512),
                            start=(cc == 0), stop=(cc == DCH - 1),
                            skip_group_check=True)),
                            deps=dep, tag=f"prj{ch}_{pair}_{pbank}_{cc}")
                        mms.append(mm)
                    xf_readers.extend(mms)
                    sq = OP("act", (lambda e, pbank=pbank: e.activation(
                        self.bf(self.P2_SQ, 512), self.bank(pbank), AF.Square)),
                        deps=[mms[-1], nprev["ss"]], tag=f"sq{ch}_{pair}_{is_q}")
                    ss = OP("pe", (lambda e, sbank=sbank: e.matmul(
                        self.psA[0:2, sbank * 512:(sbank + 1) * 512],
                        lhsT=self.ones2b_t[:], rhs=self.bf(self.P2_SQ, 512),
                        start=True, stop=True, skip_group_check=True)),
                        deps=[sq, self.n_ones2, psfree[sbank]], tag=f"ss{ch}_{pair}")
                    nprev["ss"] = ss
                    sr = OP("act", (lambda e, sbank=sbank: e.activation(
                        self.ivf, self.psA[0:2, sbank * 512:(sbank + 1) * 512],
                        AF.Sqrt)), deps=[ss, nprev["h2"], nprev["fn"]], tag="ssq")
                    m1 = OP("dve", lambda e: e.tensor_scalar_max(
                        self.ivf, self.ivf, 1e-12), deps=[sr])
                    m2 = OP("dve", lambda e: e.reciprocal(self.ivf, self.ivf),
                            deps=[m1])
                    if is_q:
                        m2 = OP("dve", lambda e: e.tensor_scalar_mul(
                            self.ivf, self.ivf, 0.125), deps=[m2])
                    h1 = OP("dve", lambda e: e.tensor_copy(self.hi2[:],
                                                           self.ivf),
                            deps=[m2, nprev["bc"]])
                    h2 = OP("dve", lambda e: e.tensor_tensor(
                        self.lo2[:], self.ivf, self.hi2[:],
                        ALU.subtract), deps=[h1])
                    nprev["h2"] = h2
                    bc0 = OP("pe", (lambda e, bbank=bbank: e.matmul(
                        self.bank(bbank), lhsT=self.selb[0:2, :], rhs=self.hi2[:],
                        start=True, stop=False, skip_group_check=True)),
                        deps=[h1, self.n_selb, psfree[bbank]], tag=f"bc0{ch}_{pair}")
                    bc = OP("pe", (lambda e, bbank=bbank: e.matmul(
                        self.bank(bbank), lhsT=self.selb[0:2, :], rhs=self.lo2[:],
                        start=False, stop=True, skip_group_check=True)),
                        deps=[h2, bc0], tag=f"bc{ch}_{pair}")
                    nprev["bc"] = bc
                    nb = OP("act", (lambda e, bbank=bbank: e.activation(
                        self.ff(self.F_NB, 512), self.bank(bbank), AF.Identity)),
                        deps=[bc, nprev["fn"]], tag="nbev")
                    psfree[bbank] = nb
                    psfree[sbank] = sr
                    if is_q:
                        tgt = self.bf(self.P2_QE, 512)
                        fn = OP("dve", (lambda e, pbank=pbank, tgt=tgt: e.tensor_tensor(
                            tgt, self.bank(pbank), self.ff(self.F_NB, 512), ALU.mult)),
                            deps=[nb, sq, qh_store.get(0)], tag=f"qn{ch}_{pair}")
                        st = DMA("pool", (lambda e, ch=ch, pair=pair: e.dma_start(
                            out=self.qhat_d[pair * 128:(pair + 1) * 128,
                                            ch * 512:(ch + 1) * 512],
                            in_=self.bf(self.P2_QE, 512))),
                            deps=[fn], tag=f"qst{ch}_{pair}", chain="qst")
                        qh_store[0] = st
                        self.qh_st[(pair, ch)] = st
                        psfree[pbank] = fn
                        nprev["fn"] = fn
                    else:
                        off = self.P2_KH + pair * T + ch * 512
                        fn = OP("dve", (lambda e, pbank=pbank, off=off: e.tensor_tensor(
                            self.bf(off, 512), self.bank(pbank),
                            self.ff(self.F_NB, 512), ALU.mult)),
                            deps=[nb, sq], tag=f"kn{ch}_{pair}")
                        kh_nodes.append(fn)
                        psfree[pbank] = fn
                        nprev["fn"] = fn
        self.p2s1_done = fl.checkpoint()
        self.kh_nodes = kh_nodes

    def p2b(self):
        fl, c = self.fl, self.cfg
        OP, DMA = fl.op, fl.dma
        D, DCH, T, NPAIR = c.D, c.DCH, c.T, c.NPAIR
        NCH = self.NCH
        bar = self.p2s1_done
        nv = self.cast_weight(self.P2_W, self.wv_d, 512, deps0=bar)
        nb_ = self.cast_weight(self.P2_W + DCH * 512, self.wb_d, 8, deps0=bar)
        xf_readers = []
        xf_ld_prev = None
        psfree = {i: None for i in range(6)}
        vh_nodes = []
        bet_prev = None
        for ch in range(NCH):
            ld = DMA("sync", (lambda e, ch=ch: e.dma_start(
                out=self.bf(self.P2_XF, DCH * 512),
                in_=self.xfs_d[:, ch * DCH * 512:(ch + 1) * DCH * 512])),
                deps=[self.xf_ready[ch], self.xf_ready[NCH - 1], xf_ld_prev]
                + xf_readers + bar,
                tag=f"xfl{ch}", chain="xfl")
            bar = []
            xf_readers = []
            for tt in range(4):
                ti = ch * 4 + tt
                bk = tt % 2
                mms = []
                for cc in range(DCH):
                    dep = [ld, nv, psfree[bk]] if cc == 0 else [ld]
                    mm = OP("pe", (lambda e, cc=cc, tt=tt, bk=bk: e.matmul(
                        self.bank(bk),
                        lhsT=self.bf(self.P2_XF + cc * 512 + tt * 128, 128),
                        rhs=self.bf(self.P2_W + cc * 512, 512),
                        start=(cc == 0), stop=(cc == DCH - 1),
                        skip_group_check=True)),
                        deps=dep, tag=f"v{ti}_{cc}")
                    mms.append(mm)
                xf_readers.extend(mms)
                evs = []
                for pair in range(NPAIR):
                    ev = OP("dve", (lambda e, pair=pair, ti=ti, bk=bk: e.tensor_copy(
                        self.bf(self.P2_VH + pair * T + ti * 128, 128),
                        self.bank(bk)[:, pair * 128:pair * 128 + 128])),
                        deps=[mms[-1]], tag=f"vev{ti}_{pair}")
                    evs.append(ev)
                vh_nodes.extend(evs)
                psfree[bk] = evs[-1]
                bms = []
                for cc in range(DCH):
                    dep = [ld, nb_, psfree[2]] if cc == 0 else [ld]
                    mm = OP("pe", (lambda e, cc=cc, tt=tt: e.matmul(
                        self.bank(2, w=8),
                        lhsT=self.bf(self.P2_XF + cc * 512 + tt * 128, 128),
                        rhs=self.bf(self.P2_W + DCH * 512 + cc * 8, 8),
                        start=(cc == 0), stop=(cc == DCH - 1),
                        skip_group_check=True)),
                        deps=dep, tag=f"be{ti}_{cc}")
                    bms.append(mm)
                xf_readers.extend(bms)
                bt = OP("act", (lambda e, ti=ti: e.activation(
                    self.af[:, self.F_BET + ti * 8:self.F_BET + ti * 8 + 8],
                    self.bank(2, w=8), AF.Sigmoid)),
                    deps=[bms[-1], bet_prev], tag=f"bsg{ti}")
                bet_prev = bt
                psfree[2] = bt
            xf_ld_prev = ld
        self.p2_done = fl.checkpoint()
        self.vh_nodes = vh_nodes

    def y_phase(self):
        fl, c = self.fl, self.cfg
        OP, DMA = fl.op, fl.dma
        D, DCH, NB, NE = c.D, c.DCH, c.NB, c.NE
        bar = self.p2_done
        y_stores = []
        wcast_prev = None
        slt_prev = None
        yev_prev = None
        bias_prev = None
        yw_readers = []
        sle_readers = []
        slt_readers = []
        yst_prev = {}
        for le in range(NE):
            wn = None
            for cc in range(DCH):
                st = self.F_WST + (cc % 2) * 1024
                ld = DMA("sync", (lambda e, le=le, cc=cc, st=st: e.dma_start(
                    out=self.ff(st, D),
                    in_=self.wexp_d[le * D + cc * 128: le * D + cc * 128 + 128, :])),
                    deps=[wcast_prev] + bar + [self.n_rs], tag=f"ywl{le}_{cc}",
                    chain=("wst", cc % 2))
                wn = OP("pool", (lambda e, cc=cc, st=st: e.tensor_copy(
                    self.bf(self.Y_W + cc * D, D), self.ff(st, D))),
                    deps=[ld] + (yw_readers if cc == 0 else []), tag=f"ywc{le}_{cc}")
                wcast_prev = wn
                if cc == 0:
                    yw_readers = []
            bld = DMA("sync", (lambda e, le=le: e.dma_start(
                out=self.af[0:32, self.F_YB:self.F_YB + D],
                in_=self.bexp_d[le * 32:(le + 1) * 32, :])),
                deps=[bias_prev] + bar, tag=f"ybl{le}", chain="ybl")
            for b in range(NB):
                sld = DMA("sync", (lambda e, le=le, b=b: e.dma_start(
                    out=self.af[0:32, self.F_SLE:self.F_SLE + D],
                    in_=self.sl_own_d[le * 32:(le + 1) * 32, b * D:(b + 1) * D])),
                    deps=[self.n_rs] + sle_readers + bar, tag=f"ysl{le}_{b}",
                    chain="ysl")
                sle_readers = []
                tps = []
                for cc in range(DCH):
                    tp = OP("pe", (lambda e, cc=cc: e.transpose(
                        self.bank(5, w=32, o=cc * 32),
                        self.af[0:32, self.F_SLE + cc * 128:
                                self.F_SLE + cc * 128 + 128],
                        self.i128f[0:32, 0:32])),
                        deps=[sld, slt_prev, self.n_cst], tag=f"ytp{le}_{b}_{cc}")
                    tps.append(tp)
                    sle_readers.append(tp)
                te = OP("dve", (lambda e: e.tensor_copy(
                    self.bf(self.Y_SLT, DCH * 32),
                    self.bank(5, w=DCH * 32))),
                    deps=tps + slt_readers, tag=f"yte{le}_{b}")
                slt_prev = te
                slt_readers = []
                prev_ev = None
                EH, EW = c.EH, c.EW
                for eh in range(EH):
                    mms = []
                    for cc in range(DCH):
                        dep = [te, wcast_prev, prev_ev] if cc == 0 else []
                        mm = OP("pe", (lambda e, cc=cc, eh=eh: e.matmul(
                            self.psA[0:32, 4 * 512:4 * 512 + EW],
                            lhsT=self.bf(self.Y_SLT + cc * 32, 32),
                            rhs=self.bf(self.Y_W + cc * D + eh * EW, EW),
                            start=(cc == 0), stop=(cc == DCH - 1),
                            skip_group_check=True)),
                            deps=dep, tag=f"ymm{le}_{b}_{eh}_{cc}")
                        mms.append(mm)
                        slt_readers.append(mm)
                        yw_readers.append(mm)
                    ev = OP("dve", (lambda e, eh=eh: e.tensor_tensor(
                        self.abf[0:32, self.Y_EV + eh * EW:self.Y_EV + eh * EW + EW],
                        self.psA[0:32, 4 * 512:4 * 512 + EW],
                        self.af[0:32, self.F_YB + eh * EW:self.F_YB + eh * EW + EW],
                        ALU.add)),
                        deps=[mms[-1], bld, yev_prev, yst_prev.get(eh)],
                        tag=f"yev{le}_{b}_{eh}")
                    yev_prev = ev
                    prev_ev = ev
                    st = DMA("pool", (lambda e, le=le, b=b, eh=eh: e.dma_start(
                        out=self.y_in_d[le * 32:(le + 1) * 32,
                                        b * D + eh * EW: b * D + eh * EW + EW],
                        in_=self.abf[0:32, self.Y_EV + eh * EW:
                                     self.Y_EV + eh * EW + EW])),
                        deps=[ev], tag=f"yst{le}_{b}_{eh}",
                        chain=("yst", eh))
                    y_stores.append(st)
                    yst_prev[eh] = st
                    bias_prev = ev
            bar = []
        self.y_done = fl.checkpoint()
        self.n_ag = fl.cc(lambda e: e.collective_compute(
            "AllGather", ALU.bypass, replica_groups=[list(range(8))],
            ins=[self.y_in_d], outs=[self.y_full_d]),
            deps=y_stores, tag="AG")

    def p3(self):
        fl, c = self.fl, self.cfg
        OP, DMA = fl.op, fl.dma
        T, NPAIR, NT, NBLK, Town = c.T, c.NPAIR, c.NT, c.NBLK, c.Town
        R, G, NSLOT, NPS = 4, 4, 8, 2
        bar = self.y_done

        tasks = []
        for tb in range(NBLK):
            for pair in range(NPAIR):
                tasks.append((pair, tb, NT - tb * R))

        bursts = []
        for ti, (pair, tb, n_s) in enumerate(tasks):
            for g0 in range(0, n_s, G):
                bursts.append((ti, g0, min(g0 + G, n_s)))

        ssb_last_av = {}
        pss_last_ev = {}
        evac_node = {}
        s_node = {}
        sa_node = {}
        av_last_of_task = {}
        oev_node = {}
        ost_node = {}
        qld_node = {}
        qld_reader = {}
        gstep = 0
        gs_of = {}
        for ti, (pair, tb, n_s) in enumerate(tasks):
            for i in range(n_s):
                gs_of[(ti, i)] = gstep
                gstep += 1
        engine_of = {}
        cd = ca = 0.0
        for ti, (pair, tb, n_s) in enumerate(tasks):
            for i in range(n_s):
                if i < R:
                    engine_of[(ti, i)] = 0
                    cd += 2100.0
                elif cd + 1225.0 <= ca + 1113.0:
                    engine_of[(ti, i)] = 0
                    cd += 1225.0
                else:
                    engine_of[(ti, i)] = 1
                    ca += 1113.0

        for ti in range(len(tasks)):
            pair, tb, n_s = tasks[ti]
            qld_node[ti] = DMA("sync", (lambda e, pair=pair, tb=tb, ti=ti: e.dma_start(
                out=self.bf(self.P3_QT + (ti % 2) * 512, 512),
                in_=self.qhat_d[pair * 128:(pair + 1) * 128,
                                tb * 512:(tb + 1) * 512])),
                deps=[self.qh_st[(pair, tb)]] + (bar if ti == 0 else []),
                tag=f"q{ti}", chain=("qld", ti % 2))

        def emit_S(bi):
            ti, i0, i1 = bursts[bi]
            pair, tb, n_s = tasks[ti]
            for i in range(i0, i1):
                gs = gs_of[(ti, i)]
                s_tile = tb * R + i
                ps_off = (gs % NPS) * 1024
                dep = [qld_node[ti]]
                if ti == 0 and i == 0:
                    dep += bar
                kh = self.P2_KH + pair * T + s_tile * 128
                qt = self.P3_QT + (ti % 2) * 512
                sa = OP("pe", (lambda e, ps_off=ps_off, kh=kh, qt=qt: e.matmul(
                    self.psA[:, ps_off:ps_off + 512],
                    lhsT=self.abf[0:64, kh:kh + 128],
                    rhs=self.abf[0:64, qt:qt + 512],
                    start=True, stop=True, tile_position=(0, 0),
                    skip_group_check=True)),
                    deps=dep, tag=f"sA{ti}_{i}")
                sa_node[(ti, i)] = sa
                n = OP("pe", (lambda e, ps_off=ps_off, kh=kh, qt=qt: e.matmul(
                    self.psA[:, ps_off + 512:ps_off + 1024],
                    lhsT=self.abf[64:128, kh:kh + 128],
                    rhs=self.abf[64:128, qt:qt + 512],
                    start=True, stop=True, tile_position=(64, 0),
                    skip_group_check=True)),
                    deps=[], tag=f"sB{ti}_{i}")
                s_node[(ti, i)] = n
                qld_reader[ti] = n

        def emit_AV(bi):
            ti, i0, i1 = bursts[bi]
            pair, tb, n_s = tasks[ti]
            for i in range(i0, i1):
                gs = gs_of[(ti, i)]
                s_tile = tb * R + i
                slot = gs % NSLOT
                sb = self.P3_SSB + slot * 1024
                vh = self.P2_VH + pair * T + s_tile * 128
                po = 4 + (ti % 2)
                dep = [evac_node[(ti, i)]]
                if i == 0:
                    dep += [oev_node.get(ti - 2)]
                OP("pe", (lambda e, sb=sb, vh=vh, po=po, i=i, n_s=n_s: e.matmul(
                    self.psA[0:64, po * 512:po * 512 + 512],
                    lhsT=self.abf[:, vh:vh + 64],
                    rhs=self.bf(sb, 512),
                    start=(i == 0), stop=(i == n_s - 1),
                    tile_position=(0, 0), skip_group_check=True)),
                    deps=dep, tag=f"avA{ti}_{i}")
                n = OP("pe", (lambda e, sb=sb, vh=vh, po=po, i=i, n_s=n_s: e.matmul(
                    self.psA[64:128, po * 512:po * 512 + 512],
                    lhsT=self.abf[:, vh + 64:vh + 128],
                    rhs=self.bf(sb + 512, 512),
                    start=(i == 0), stop=(i == n_s - 1),
                    tile_position=(0, 64), skip_group_check=True)),
                    deps=[], tag=f"avB{ti}_{i}")
                av_last_of_task[ti] = n
                ssb_last_av[slot] = n

        def emit_evac(ti, i):
            pair, tb, n_s = tasks[ti]
            gs = gs_of[(ti, i)]
            s_tile = tb * R + i
            slot = gs % NSLOT
            ps_off = (gs % NPS) * 1024
            sb = self.P3_SSB + slot * 1024
            eng = "dve" if engine_of[(ti, i)] == 0 else "act"
            dep = [s_node[(ti, i)]]
            prev_user = gs - NSLOT
            if prev_user >= 0:
                dep.append(self._av_by_gs.get(prev_user))
            if eng == "dve" and i < R:
                k = i
                t1 = OP("dve", (lambda e, sb=sb, ps_off=ps_off, k=k: e.tensor_tensor(
                    self.bf(sb, 512), self.psA[:, ps_off:ps_off + 512],
                    self.mask(k), ALU.mult)),
                    deps=dep + [self.n_cst], tag=f"evm{ti}_{i}")
                t2 = OP("dve", (lambda e, sb=sb, ps_off=ps_off, k=k: e.tensor_tensor(
                    self.bf(sb + 512, 512),
                    self.psA[:, ps_off + 512:ps_off + 1024],
                    self.mask(k), ALU.mult)),
                    deps=dep, tag=f"evm2{ti}_{i}")
                bA = self.af[:, self.F_BET + s_tile * 8 + 2 * pair:
                             self.F_BET + s_tile * 8 + 2 * pair + 1]
                bB = self.af[:, self.F_BET + s_tile * 8 + 2 * pair + 1:
                             self.F_BET + s_tile * 8 + 2 * pair + 2]
                t3 = OP("dve", (lambda e, sb=sb, k=k, bA=bA: e.scalar_tensor_tensor(
                    out=self.bf(sb + 128 * k, 128), in0=self.i128b, scalar=bA,
                    in1=self.bf(sb + 128 * k, 128), op0=ALU.mult, op1=ALU.add)),
                    deps=[t1, self.n_i128], tag=f"evb{ti}_{i}")
                t4 = OP("dve", (lambda e, sb=sb, k=k, bB=bB: e.scalar_tensor_tensor(
                    out=self.bf(sb + 512 + 128 * k, 128), in0=self.i128b, scalar=bB,
                    in1=self.bf(sb + 512 + 128 * k, 128), op0=ALU.mult, op1=ALU.add)),
                    deps=[t2, t3], tag=f"evb2{ti}_{i}")
                evac_node[(ti, i)] = t4
            elif eng == "dve":
                n = OP("dve", (lambda e, sb=sb, ps_off=ps_off: e.tensor_copy(
                    self.bf(sb, 1024), self.psA[:, ps_off:ps_off + 1024])),
                    deps=dep, tag=f"evd{ti}_{i}")
                evac_node[(ti, i)] = n
            else:
                n = OP("act", (lambda e, sb=sb, ps_off=ps_off: e.copy(
                    self.bf(sb, 1024), self.psA[:, ps_off:ps_off + 1024])),
                    deps=dep, tag=f"eva{ti}_{i}")
                evac_node[(ti, i)] = n

        self._av_by_gs = {}

        def emit_oev(ti):
            pair, tb, n_s = tasks[ti]
            po = 4 + (ti % 2)
            oe = OP("dve", (lambda e, po=po, ti=ti: e.tensor_copy(
                self.bf(self.P3_OE + (ti % 2) * 512, 512),
                self.psA[:, po * 512:po * 512 + 512])),
                deps=[av_last_of_task[ti], ost_node.get(ti - 2)],
                tag=f"oev{ti}")
            oev_node[ti] = oe
            w = min(512, Town)
            st = None
            for off in range(0, 512, w):
                t0 = tb * 512 + off
                dest, col = t0 // Town, t0 % Town
                st = DMA("pool", (lambda e, dest=dest, col=col, pair=pair,
                                  ti=ti, off=off, w=w: e.dma_start(
                    out=self.oh_send_d[dest * 512 + pair * 128:
                                       dest * 512 + pair * 128 + 128,
                                       col:col + w],
                    in_=self.bf(self.P3_OE + (ti % 2) * 512 + off, w))),
                    deps=[oe], tag=f"ost{ti}_{off}",
                    chain=("ost", ti % 2))
            ost_node[ti] = st

        pending_oev = []
        emit_S(0)
        for bi in range(len(bursts)):
            ti, i0, i1 = bursts[bi]
            for i in range(i0, i1):
                emit_evac(ti, i)
            for pt in pending_oev:
                emit_oev(pt)
            pending_oev = []
            if bi + 1 < len(bursts):
                emit_S(bi + 1)
            emit_AV(bi)
            for i in range(i0, i1):
                self._av_by_gs[gs_of[(ti, i)]] = av_last_of_task[ti]
            for i in range(i0, i1):
                pss_last_ev[gs_of[(ti, i)]] = evac_node[(ti, i)]
            if i1 == tasks[ti][2]:
                pending_oev.append(ti)
        for pt in pending_oev:
            emit_oev(pt)
        ev_by_gs = {}
        s_reader_last = {}
        for ti2, (pair2, tb2, n_s2) in enumerate(tasks):
            for i in range(n_s2):
                ev_by_gs[gs_of[(ti2, i)]] = evac_node[(ti2, i)]
        for ti2, (pair2, tb2, n_s2) in enumerate(tasks):
            for i in range(n_s2):
                gs = gs_of[(ti2, i)]
                if gs - NPS >= 0:
                    sa_node[(ti2, i)].deps.append(ev_by_gs[gs - NPS])
        for ti2 in range(2, len(tasks)):
            qld_node[ti2].deps.append(qld_reader[ti2 - 2])
        def _a2a(e):
            with allow_ici_all_to_all():
                return e.collective_compute(
                    "AllToAll", ALU.bypass, replica_groups=[list(range(8))],
                    ins=[self.oh_send_d], outs=[self.oh_recv_d])
        self.n_a2a = fl.cc(_a2a, deps=list(ost_node.values()), tag="A2A")
        self.p3_done = fl.checkpoint()

    def p4(self):
        fl, c = self.fl, self.cfg
        OP, DMA = fl.op, fl.dma
        D, DCH, NB, NMT, NP, NPT = c.D, c.DCH, c.NB, c.NMT, c.NP, c.NPT
        Town = c.Town
        bar = self.p3_done
        nwo = self.cast_weight(self.P4_WO, self.wo_d, D, deps0=bar,
                               nch=c.HCH)
        ndcr = OP("dve", lambda e: e.reciprocal(
            self.ff(self.F_DCR, NB * NMT), self.ff(self.F_DC, NB * NMT)),
            deps=bar, tag="dcr")
        er_reader = {0: None, 1: None}
        ct_reader = None
        cb_reader = None
        bank_free = {i: None for i in range(6)}
        psb_free = {0: None, 1: None}
        g_reader = None
        fo_reader = None
        xld_reader = {0: None, 1: None}
        ysb_nodes = {}
        oh_nodes = {}
        out_stores = []
        for b in range(NB):
            yl = []
            for npt in range(NPT):
                n = DMA("sync", (lambda e, b=b, npt=npt: e.dma_start(
                    out=self.bf(self.P4_Y + npt * D, D),
                    in_=self.y_full_d[npt * 128:(npt + 1) * 128, b * D:(b + 1) * D])),
                    deps=[self.n_ag] + bar + [ysb_nodes.get(b - 1)],
                    tag=f"yld{b}_{npt}", chain="yld")
                yl.append(n)
            ohl = []
            for cc in range(c.HCH):
                n = DMA("sync", (lambda e, b=b, cc=cc: e.dma_start(
                    out=self.bf(self.P4_OH + cc * Town, Town),
                    in_=self.oh_recv_d[b * 1024 + cc * 128: b * 1024 + cc * 128 + 128,
                                       :])),
                    deps=[self.n_a2a] + bar + [oh_nodes.get(b - 1)],
                    tag=f"ohl{b}_{cc}", chain="ohl")
                ohl.append(n)
            bar = []
            last_smm = None
            last_cmm = None
            for mt in range(NMT):
                j = b * NMT + mt
                sl = j % 2
                erl = DMA("sync", (lambda e, b=b, mt=mt, sl=sl: e.dma_start(
                    out=self.bf(self.P4_ER + sl * NP, NP),
                    in_=self.e_sp_d[b * Town + mt * 128: b * Town + mt * 128 + 128, :])),
                    deps=[er_reader[sl]] + self.e_st[(b, mt)], tag=f"erl{j}",
                    chain=("erl", sl))
                dcr = self.af[:, self.F_DCR + b * NMT + mt:
                              self.F_DCR + b * NMT + mt + 1]
                cb = OP("dve", (lambda e, sl=sl, dcr=dcr: e.tensor_scalar_mul(
                    self.bf(self.P4_CB, NP), self.bf(self.P4_ER + sl * NP, NP), dcr)),
                    deps=[erl, ndcr, cb_reader], tag=f"cb{j}")
                er_reader[sl] = cb
                tps = {0: [], 1: []}
                for npt in range(NPT):
                    h = npt // (NPT // 2)
                    n = OP("pe", (lambda e, npt=npt, h=h: e.transpose(
                        self.psB[:, h * 1024 + (npt % (NPT // 2)) * 128:
                                 h * 1024 + (npt % (NPT // 2)) * 128 + 128],
                        self.bf(self.P4_CB + npt * 128, 128), self.i128b)),
                        deps=[cb, psb_free[h], self.n_i128], tag=f"ctp{j}_{npt}")
                    tps[h].append(n)
                ct_evs = []
                for h in (0, 1):
                    if not tps[h]:
                        continue
                    w = len(tps[h]) * 128
                    n = OP("dve", (lambda e, h=h, w=w: e.tensor_copy(
                        self.bf(self.P4_CT + h * (NPT // 2) * 128, w),
                        self.psB[:, h * 1024:h * 1024 + w])),
                        deps=tps[h] + ([ct_reader] if ct_reader else []),
                        tag=f"cte{j}_{h}")
                    psb_free[h] = n
                    ct_evs.append(n)
                cb_reader = tps[0][-1] if not tps[1] else tps[1][-1]
                cmms = []
                EH, EW = c.EH, c.EW
                for eh in range(EH):
                    for npt in range(NPT):
                        dep = []
                        if npt == 0:
                            dep = ct_evs + yl + [bank_free[eh]]
                        mm = OP("pe", (lambda e, b=b, npt=npt, eh=eh: e.matmul(
                            self.bank(eh, w=EW),
                            lhsT=self.bf(self.P4_CT + npt * 128, 128),
                            rhs=self.bf(self.P4_Y + npt * D + eh * EW, EW),
                            start=(npt == 0), stop=(npt == NPT - 1),
                            skip_group_check=True)),
                            deps=dep, tag=f"cm{j}_{eh}_{npt}")
                        cmms.append(mm)
                last_cmm = cmms[-1]
                ct_reader = last_cmm
                gn = []
                for eh in range(EH):
                    n = OP("act", (lambda e, eh=eh: e.activation(
                        self.ff(self.F_G + eh * EW, EW), self.bank(eh, w=EW),
                        AF.Silu)),
                        deps=[cmms[(eh + 1) * NPT - 1], g_reader], tag=f"g{j}_{eh}")
                    bank_free[eh] = n
                    gn.append(n)
                smms = []
                for eh in range(EH):
                    for cc in range(c.HCH):
                        dep = []
                        if cc == 0:
                            dep = ohl + [nwo, bank_free[2 + eh]]
                        mm = OP("pe", (lambda e, cc=cc, eh=eh, mt=mt: e.matmul(
                            self.bank(2 + eh, w=EW),
                            lhsT=self.bf(self.P4_OH + cc * Town + mt * 128, 128),
                            rhs=self.bf(self.P4_WO + cc * D + eh * EW, EW),
                            start=(cc == 0), stop=(cc == c.HCH - 1),
                            skip_group_check=True)),
                            deps=dep, tag=f"sm{j}_{eh}_{cc}")
                        smms.append(mm)
                last_smm = smms[-1]
                xl = DMA("sync", (lambda e, b=b, mt=mt, sl=sl: e.dma_start(
                    out=self.ff(self.F_XLD + sl * D, D),
                    in_=self.xo_d[b * Town + mt * 128: b * Town + mt * 128 + 128, :])),
                    deps=[xld_reader[sl]] + (self.p3_done if j < 2 else []),
                    tag=f"p4x{j}", chain=("xld", sl))
                f1 = OP("dve", (lambda e: e.tensor_tensor(
                    self.ff(self.F_FO, D), self.psA[:, 2 * 512:2 * 512 + D],
                    self.ff(self.F_G, D), ALU.mult)),
                    deps=[smms[c.HCH - 1], smms[-1]] + gn + [fo_reader],
                    tag=f"f1{j}")
                bank_free[2] = f1
                bank_free[3] = f1
                g_reader = f1
                f2 = OP("dve", (lambda e, sl=sl: e.tensor_tensor(
                    self.ff(self.F_FO, D), self.ff(self.F_FO, D),
                    self.ff(self.F_XLD + sl * D, D), ALU.add)),
                    deps=[f1, xl], tag=f"f2{j}")
                xld_reader[sl] = f2
                st = DMA("act", (lambda e, b=b, mt=mt: e.dma_start(
                    out=self.out_d[b * Town + mt * 128: b * Town + mt * 128 + 128, :],
                    in_=self.ff(self.F_FO, D))),
                    deps=[f2], tag=f"ost{j}", chain="outst")
                fo_reader = st
                out_stores.append(st)
            ysb_nodes[b] = last_cmm
            oh_nodes[b] = last_smm
        fl.op("pool", lambda e: e.nop(), deps=out_stores, tag="fin")

    def build(self, nphases=8):
        import os
        phases = [self.p0, self.p1a, self.p1b, self.p2, self.p2b,
                  self.y_phase, self.p3, self.p4]
        for ph in phases[:nphases]:
            ph()
        if os.environ.get("RSONLY"):
            self.fl.cc(lambda e: e.collective_compute(
                "ReduceScatter", ALU.add, replica_groups=[list(range(8))],
                ins=[self.sl_in_d], outs=[self.sl_own_d]), deps=[], tag="RSX")
        fl = self.fl
        fl.finalize()
        with self.nc.Block() as block:
            @block.tensor
            def _(t):
                fl.emit("pe", t)

            @block.vector
            def _(v):
                fl.emit("dve", v)

            @block.scalar
            def _(s):
                fl.emit("act", s)

            @block.gpsimd
            def _(g):
                fl.emit("pool", g)

            @block.sync
            def _(sy):
                fl.emit("sync", sy)
        return self.nc


def shard_inputs(cfg, L0_attn_x, L0_attn_Wk, L0_attn_Wv, L0_attn_Wq,
                 L0_attn_Wbeta, L0_attn_Wout, L0_ffn_phi, L0_ffn_W, L0_ffn_b):
    c = cfg
    x = np.ascontiguousarray(np.asarray(L0_attn_x, dtype=np.float32))
    cst = build_consts(c)
    sel = build_sel(c)
    maps = []
    for core in range(8):
        b, sg = core // 2, core % 2
        m = {
            "xb": np.ascontiguousarray(x[b]),
            "xo": np.ascontiguousarray(
                x[:, core * c.Town:(core + 1) * c.Town, :]).reshape(-1, c.D),
            "wk": np.ascontiguousarray(L0_attn_Wk[:, sg * 512:(sg + 1) * 512]),
            "wq": np.ascontiguousarray(L0_attn_Wq[:, sg * 512:(sg + 1) * 512]),
            "wv": np.ascontiguousarray(L0_attn_Wv[:, sg * 512:(sg + 1) * 512]),
            "wb": np.ascontiguousarray(L0_attn_Wbeta[:, sg * 8:(sg + 1) * 8]),
            "wo": np.ascontiguousarray(L0_attn_Wout),
            "phi": np.ascontiguousarray(L0_ffn_phi.reshape(c.D, c.NP)),
            "wexp": np.ascontiguousarray(
                L0_ffn_W[core * c.NE:(core + 1) * c.NE]).reshape(c.NE * c.D, c.D),
            "bexp": np.ascontiguousarray(np.repeat(
                L0_ffn_b[core * c.NE:(core + 1) * c.NE], 32, axis=0)),
            "cst": cst,
            "sel": sel,
        }
        maps.append({k: np.asarray(v, dtype=np.float32) for k, v in m.items()})
    return maps


def assemble_output(cfg, results):
    c = cfg
    out = np.zeros((c.NB, c.T, c.D), np.float32)
    for core in range(8):
        o = np.asarray(results[core]["out"], dtype=np.float32).reshape(
            c.NB, c.Town, c.D)
        out[:, core * c.Town:(core + 1) * c.Town, :] = o
    return out


def kernel(L0_attn_x, L0_attn_Wk, L0_attn_Wv, L0_attn_Wq, L0_attn_Wbeta,
           L0_attn_Wout, L0_ffn_phi, L0_ffn_W, L0_ffn_b):
    from concourse.bass_utils import run_bass_kernel_spmd
    cfg = CFG(T=8192, D=1024, NEXP=64)
    b = Builder(cfg)
    nc = b.build()
    maps = shard_inputs(cfg, L0_attn_x, L0_attn_Wk, L0_attn_Wv, L0_attn_Wq,
                        L0_attn_Wbeta, L0_attn_Wout, L0_ffn_phi, L0_ffn_W,
                        L0_ffn_b)
    res = run_bass_kernel_spmd(nc, maps, core_ids=list(range(8)))
    return assemble_output(cfg, res.results)
